# version 52
# baseline (speedup 1.0000x reference)
import numpy as np
import ml_dtypes
from contextlib import ExitStack

import concourse.bass as bass
import concourse.mybir as mybir
from concourse.bass_utils import run_bass_kernel_spmd

F32 = mybir.dt.float32
BF16 = mybir.dt.bfloat16
AF = mybir.ActivationFunctionType
ALU = mybir.AluOpType
AX = mybir.AxisListType

NCORES = 8
TOK = 2048
D = 1024
NT = TOK // 128
NH = 16
HD = 64
SCALE = 8.0 / HD
EPS = 1e-5


def schedule(nt):
    Vs = [(0, 'bnst'), (0, 'aggr'), (0, 'xn'), (-1, 'qTzms'), (-1, 'PTzms')]
    Vs += [(-1, f'wsum{c}') for c in range(8)]
    Ss = [(0, 'lnx'), (0, 'rstdx')]
    Ts = []
    Gs = []
    for i in range(nt):
        if i == 0:
            Vs += [(1, 'bnst'), (1, 'aggr'), (1, 'xn'), (0, 'red0'), (0, 'red1'),
                   (0, 'red2'), (0, 'red3'), (0, 'mean'), (0, 'varq'), (0, 'vark'),
                   (0, 'qcen'), (0, 'qmul'), (0, 'kmul'), (0, 'qTe')]
        else:
            Vs += [(i - 1, 'negM0'), (i - 1, 'negM1'), (i - 1, 'sum0'), (i - 1, 'sum1'),
                   (i - 1, 'rcp'), (i - 1, 'normw'),
                   (i, 'mean'), (i, 'red0'), (i, 'red1'), (i, 'varq'), (i, 'qcen'), (i, 'qmul')]
            if i + 1 < nt:
                Vs += [(i + 1, 'bnst'), (i + 1, 'aggr')]
            Vs += [(i, 'red2'), (i, 'red3'), (i, 'vark')]
            if i + 1 < nt:
                Vs += [(i + 1, 'xn')]
            Vs += [(i, 'kmul'), (i, 'qTe')]
            Vs += [(i - 1, 'sb')]
        if i == 0:
            Ss += [(0, 'xnTe'), (0, 'evac0'), (0, 'evac1'), (-1, 'sign'),
                   (1, 'lnx'), (1, 'rstdx'),
                   (0, 'evac2'), (0, 'evac3'),
                   (0, 'lnq'), (0, 'rsq'), (0, 'lnk'), (0, 'rsk'),
                   (0, 'evac4'), (0, 'evac5'), (0, 'silu0'), (0, 'silu1'), (0, 'kTe'),
                   (1, 'xnTe')]
        else:
            Ss += [(i - 1, 'exp0'), (i, 'evac0'), (i - 1, 'exp1'), (i, 'evac1'),
                   (i, 'evac2'), (i - 1, 'PTe'), (i, 'evac3'),
                   (i, 'lnq'), (i, 'rsq'), (i - 1, 'attnTe')]
            if i + 1 < nt:
                Ss += [(i + 1, 'lnx'), (i + 1, 'rstdx')]
            Ss += [(i, 'lnk'), (i, 'rsk'), (i, 'silu0'), (i, 'silu1'), (i, 'kTe')]
            if i + 1 < nt:
                Ss += [(i + 1, 'xnTe')]
            Ss += [(i, 'evac4'), (i, 'evac5')]
        if i == 0:
            Ts += [(0, 'xnT'), (0, 'qkv0'), (0, 'qkv1'), (0, 'qkv2'), (0, 'qkv3'),
                   (0, 'qkv4'), (0, 'qkv5'), (0, 'stats'), (0, 'ffn0'), (0, 'ffn1'),
                   (0, 'qnTq'), (0, 'qnTk'), (1, 'xnT'), (0, 'scores')]
        else:
            Ts += [(i, 'qkv0'), (i, 'stats'), (i, 'qkv1'), (i, 'qkv2'), (i, 'qkv3'),
                   (i - 1, 'PT'), (i, 'ffn0'), (i - 1, 'av'), (i, 'ffn1'),
                   (i, 'qnTq'), (i - 1, 'proj')]
            if i + 1 < nt:
                Ts += [(i + 1, 'xnT')]
            Ts += [(i, 'qnTk'), (i, 'qkv4'), (i, 'qkv5'), (i, 'scores')]
        if i == 0:
            Gs += [(0, 'sq0'), (0, 'sq1'), (0, 'sq2'), (0, 'sq3'), (0, 'kcen')]
        else:
            Gs += [(i, 'sq0'), (i, 'sq1'), (i, 'sq2'), (i, 'sq3'), (i, 'kcen'),
                   (i - 1, 'gate2'), (i - 1, 'gate')]
    t = nt - 1
    Vs += [(t, e) for e in ('negM0', 'negM1', 'sum0', 'sum1', 'rcp', 'normw', 'sb')]
    Ss += [(t, 'exp0'), (t, 'exp1'), (t, 'PTe'), (t, 'attnTe')]
    Ts += [(t, 'PT'), (t, 'av'), (t, 'proj')]
    Gs += [(t, 'gate2'), (t, 'gate')]
    return Vs, Ss, Ts, Gs


def build_nc(nt=NT):
    tok = nt * 128
    Vs, Ss, Ts, Gs = schedule(nt)
    CV = {ev: i + 1 for i, ev in enumerate(Vs)}
    CS = {ev: i + 1 for i, ev in enumerate(Ss)}
    CT = {ev: i + 1 for i, ev in enumerate(Ts)}
    CG = {ev: i + 1 for i, ev in enumerate(Gs)}

    def V(t, e):
        return CV[(t, e)]

    def S(t, e):
        return CS[(t, e)]

    def T(t, e):
        return CT[(t, e)]

    def G(t, e):
        return CG[(t, e)]

    nc = bass.Bass("TRN2", target_bir_lowering=False)
    x_d = nc.dram_tensor("x", [tok, D], F32, kind="ExternalInput").ap()
    wqkv_d = nc.dram_tensor("wqkv", [D, 3 * D], F32, kind="ExternalInput").ap()
    wproj_d = nc.dram_tensor("wproj", [D, D], F32, kind="ExternalInput").ap()
    wffnT_d = nc.dram_tensor("wffnT", [D, D], F32, kind="ExternalInput").ap()
    bproj_d = nc.dram_tensor("bproj", [128, D], F32, kind="ExternalInput").ap()
    eye_d = nc.dram_tensor("eye", [128, 128], BF16, kind="ExternalInput").ap()
    cst_d = nc.dram_tensor("cst", [128, 2], F32, kind="ExternalInput").ap()
    gb4_d = nc.dram_tensor("gb4", [128, 4], F32, kind="ExternalInput").ap()
    y_d = nc.dram_tensor("y", [tok, D], F32, kind="ExternalOutput").ap()

    with ExitStack() as st_:
        E = st_.enter_context
        wqkv = E(nc.sbuf_tensor("wqkv_sb", [128, 8, 3 * D], BF16))
        wsum = E(nc.sbuf_tensor("wsum_sb", [128, 8, 32], BF16))
        wproj = E(nc.sbuf_tensor("wproj_sb", [128, 8, D], BF16))
        wfs = E(nc.sbuf_tensor("wfs_sb", [128, 8 * D], BF16))
        bprojf = E(nc.sbuf_tensor("bprojf_sb", [128, D], F32))
        eye = E(nc.sbuf_tensor("eye_sb", [128, 128], BF16))
        cst = E(nc.sbuf_tensor("cst_sb", [128, 2], F32))
        gb4 = E(nc.sbuf_tensor("gb4_sb", [128, 4], F32))
        x_sb = [E(nc.sbuf_tensor(f"x_sb{i}", [128, D], F32)) for i in range(6)]
        out_sb = [E(nc.sbuf_tensor(f"out_sb{i}", [128, D], F32)) for i in range(2)]
        xnbf = [E(nc.sbuf_tensor(f"xnbf{i}", [128, D], BF16)) for i in range(2)]
        xnT = [E(nc.sbuf_tensor(f"xnT{i}", [128, D], BF16)) for i in range(2)]
        qk_sb = [E(nc.sbuf_tensor(f"qk_sb{i}", [128, 2048], F32)) for i in range(2)]
        v_sb = [E(nc.sbuf_tensor(f"v_sb{i}", [128, D], BF16)) for i in range(2)]
        g_sb = [E(nc.sbuf_tensor(f"g_sb{i}", [128, D], BF16)) for i in range(2)]
        sq_sb = E(nc.sbuf_tensor("sq_sb", [128, 2048], BF16))
        qkn = E(nc.sbuf_tensor("qkn", [128, 2048], BF16))
        qTz = E(nc.sbuf_tensor("qTz", [128, 2048], BF16))
        kT_sb = E(nc.sbuf_tensor("kT_sb", [128, 1024], BF16))
        E_sb = E(nc.sbuf_tensor("E_sb", [128, D], BF16))
        PTz_sb = E(nc.sbuf_tensor("PTz_sb", [128, 2048], BF16))
        at_sb = E(nc.sbuf_tensor("at_sb", [128, D], BF16))
        s_bf = E(nc.sbuf_tensor("s_bf", [128, D], BF16))
        sg_sb = E(nc.sbuf_tensor("sg_sb", [128, D], BF16))
        st6 = E(nc.sbuf_tensor("st6", [128, 12], F32))
        mv = E(nc.sbuf_tensor("mv", [128, 2], F32))
        lnx = E(nc.sbuf_tensor("lnx", [128, 1], F32))
        rstdx = [E(nc.sbuf_tensor(f"rstdx{i}", [128, 1], F32)) for i in range(2)]
        negmr = E(nc.sbuf_tensor("negmr", [128, 1], F32))
        sumsq = E(nc.sbuf_tensor("sumsq", [128, 32], F32))
        m_qk = [E(nc.sbuf_tensor(f"m_qk{i}", [128, 32], F32)) for i in range(2)]
        m2 = E(nc.sbuf_tensor("m2", [128, 32], F32))
        var_qk = E(nc.sbuf_tensor("var_qk", [128, 32], F32))
        ln32 = E(nc.sbuf_tensor("ln32", [128, 32], F32))
        rstdqk = [E(nc.sbuf_tensor(f"rstdqk{i}", [128, 32], F32)) for i in range(2)]
        negM = E(nc.sbuf_tensor("negM", [128, 2], F32))
        dum = E(nc.sbuf_tensor("dum", [128, 1], F32))
        S16 = E(nc.sbuf_tensor("S16", [128, 16], F32))
        r16 = E(nc.sbuf_tensor("r16", [128, 16], F32))
        P_A = E(nc.psum_tensor("P_A", [128, 1024], F32))
        P_B = E(nc.psum_tensor("P_B", [128, 2048], BF16))
        P_C = E(nc.psum_tensor("P_C", [128, 1024], F32))
        P_ST = E(nc.psum_tensor("P_ST", [128, 512], F32))
        P_XT = E(nc.psum_tensor("P_XT", [128, 1024], BF16))
        qkc = E(nc.sbuf_tensor("qkc_sb", [128, 4096], F32))

        SV = E(nc.semaphore("SV"))
        SS = E(nc.semaphore("SS"))
        ST = E(nc.semaphore("ST"))
        SPl = E(nc.semaphore("SPl"))
        SP = E(nc.semaphore("SP"))
        SW1s = [E(nc.semaphore(f"SW1_{i}")) for i in range(6)]
        SW3 = E(nc.semaphore("SW3"))
        SW4 = E(nc.semaphore("SW4"))
        SXs = [E(nc.semaphore(f"SX{i}")) for i in range(6)]
        SYs = [E(nc.semaphore(f"SY{i}")) for i in range(2)]
        SC = E(nc.semaphore("SC"))
        SCc = E(nc.semaphore("SCc"))
        SCe = E(nc.semaphore("SCe"))
        SVi = E(nc.semaphore("SVi"))
        SSi = E(nc.semaphore("SSi"))
        block = E(nc.Block())

        @block.sync
        def _(sync):
            sync.dma_start(out=x_sb[0][:, :], in_=x_d[0:128, :]).then_inc(SXs[0], 16)
            sync.dma_start(out=cst[:, :], in_=cst_d).then_inc(SCc, 16)
            sync.dma_start(out=eye[:, :], in_=eye_d).then_inc(SCe, 16)
            sync.dma_start(out=gb4[:, :], in_=gb4_d).then_inc(SC, 16)
            sync.dma_start(out=bprojf[:, :], in_=bproj_d).then_inc(SC, 16)
            for u in range(1, min(6, nt)):
                sync.dma_start(out=x_sb[u % 6][:, :], in_=x_d[u * 128:(u + 1) * 128, :]).then_inc(SXs[u % 6], 16)
            for t in range(nt):
                sync.wait_ge(SPl, G(t, 'gate'))
                sync.wait_ge(SV, V(t, 'xn'))
                sync.dma_start(out=y_d[t * 128:(t + 1) * 128, :], in_=out_sb[t % 2][:, :]).then_inc(SYs[t % 2], 16)
                u = t + 6
                if u < nt:
                    sync.dma_start(out=x_sb[u % 6][:, :], in_=x_d[u * 128:(u + 1) * 128, :]).then_inc(SXs[u % 6], 16)
            sync.wait_ge(SYs[0], 16 * ((nt + 1) // 2))
            sync.wait_ge(SYs[1], 16 * (nt // 2))

        @block.gpsimd
        def _(gpsimd):
            wqkv_r = wqkv_d.rearrange("(c p) m -> p c m", p=128)
            for n in range(6):
                gpsimd.dma_start(out=wqkv[:, :, n * 512:(n + 1) * 512],
                                 in_=wqkv_r[:, :, n * 512:(n + 1) * 512]).then_inc(SW1s[n], 16)
            gpsimd.dma_start(out=wproj[:, :, :],
                             in_=wproj_d.rearrange("(c p) m -> p c m", p=128)).then_inc(SW4, 16)
            gpsimd.wait_ge(SC, 32)
            gpsimd.tensor_scalar(out=gb4[:, 0:2], in0=gb4[:, 0:2], scalar1=SCALE, scalar2=None, op0=ALU.mult).then_inc(SP, 1)

            _gpos = [0]

            def ev(name, t, ins):
                assert CG[(t, name)] == _gpos[0] + 1, (name, t)
                _gpos[0] += 1
                ins.then_inc(SPl, 1)

            def e_sq(t, g):
                gpsimd.wait_ge(SS, S(t, f'evac{g}'))
                if t >= 1:
                    gpsimd.wait_ge(SV, V(t - 1, f'red{g}'))
                src = qk_sb[t % 2][:, g * 512:(g + 1) * 512]
                ev(f'sq{g}', t, gpsimd.tensor_tensor(
                    out=sq_sb[:, g * 512:(g + 1) * 512], in0=src, in1=src, op=ALU.mult))

            def e_kcen(t):
                gpsimd.wait_ge(SV, V(t, 'mean'))
                gpsimd.wait_ge(SS, S(-1, 'sign'))
                ev('kcen', t, gpsimd.tensor_tensor(
                    out=qkc[:, 1024:2048].rearrange("p (h d) -> p h d", d=64),
                    in0=qk_sb[t % 2][:, 1024:2048].rearrange("p (h d) -> p h d", d=64),
                    in1=m_qk[t % 2][:, 16:32, None].broadcast_to([128, 16, 64]),
                    op=ALU.subtract))

            def e_gate2(t):
                gpsimd.wait_ge(SV, V(t, 'sb'))
                gpsimd.wait_ge(SS, S(t, 'silu1'))
                if t >= 1:
                    gpsimd.wait_ge(SPl, G(t - 1, 'gate'))
                ev('gate2', t, gpsimd.tensor_tensor(out=sg_sb[:, :], in0=s_bf[:, :],
                                                    in1=g_sb[t % 2][:, :], op=ALU.mult))

            def e_gate(t):
                xt = x_sb[t % 6]
                gpsimd.wait_ge(SPl, G(t, 'gate2'))
                if t >= 2:
                    gpsimd.wait_ge(SYs[t % 2], 16 * ((t - 2) // 2 + 1))
                ev('gate', t, gpsimd.tensor_tensor(out=out_sb[t % 2][:, :], in0=sg_sb[:, :],
                                                   in1=xt[:, :], op=ALU.add))

            gdisp = {'kcen': e_kcen, 'gate2': e_gate2, 'gate': e_gate}
            for (t, name) in Gs:
                if name.startswith('sq'):
                    e_sq(t, int(name[2:]))
                else:
                    gdisp[name](t)

        @block.vector
        def _(vector):
            vcnt = [0]

            def vsync(ins):
                ins.then_inc(SVi, 1)
                vcnt[0] += 1
                vector.wait_ge(SVi, vcnt[0])

            def ev(name, t, ins):
                assert CV[(t, name)] == _vpos[0] + 1, (name, t)
                _vpos[0] += 1
                ins.then_inc(SV, 1)
            _vpos = [0]

            def e_qTzms(t):
                ev('qTzms', t, vector.memset(qTz[:, :], 0.0))

            def e_PTzms(t):
                ev('PTzms', t, vector.memset(PTz_sb[:, :], 0.0))

            def e_wsum(t, c):
                if c == 0:
                    for n in range(4):
                        vector.wait_ge(SW1s[n], 16)
                with nc.allow_low_precision(reason="wsum rhs must be bf16 for matmul; reduce math is fp32"):
                    ev(f'wsum{c}', t, vector.tensor_reduce(
                        out=wsum[:, c, :],
                        in_=wqkv[:, c, 0:2048].rearrange("p (h d) -> p h d", d=64),
                        axis=AX.X, op=ALU.add))

            def e_bnst(t):
                xt = x_sb[t % 6]
                vector.wait_ge(SXs[t % 6], 16 * (t // 6 + 1))
                vector.bn_stats(out=st6[:, 0:6], in_=xt[:, 0:512])
                ev('bnst', t, vector.bn_stats(out=st6[:, 6:12], in_=xt[:, 512:1024]))

            def e_aggr(t):
                vector.wait_ge(SV, V(t, 'bnst'))
                if t >= 1:
                    vector.wait_ge(SS, S(t - 1, 'lnx'))
                ev('aggr', t, vector.bn_aggr(out=mv[:, :], in_=st6[:, :]))

            def e_xn(t):
                xt = x_sb[t % 6]
                vector.wait_ge(SS, S(t, 'rstdx'))
                if t >= 2:
                    vector.wait_ge(ST, T(t - 2, 'xnT'))
                vsync(vector.tensor_scalar(out=negmr[:, :], in0=mv[:, 0:1], scalar1=rstdx[t % 2][:, :],
                                           scalar2=-1.0, op0=ALU.mult, op1=ALU.mult))
                ev('xn', t, vector.tensor_scalar(out=xnbf[t % 2][:, :], in0=xt[:, :], scalar1=rstdx[t % 2][:, :],
                                                 scalar2=negmr[:, :], op0=ALU.mult, op1=ALU.add))

            def e_mean(t):
                vector.wait_ge(ST, T(t, 'stats'))
                if t >= 2:
                    vector.wait_ge(SPl, G(t - 2, 'kcen'))
                ev('mean', t, vector.tensor_scalar(out=m_qk[t % 2][:, :], in0=P_ST[:, 0:32],
                                                   scalar1=1.0 / HD, scalar2=None, op0=ALU.mult))

            def e_red(t, g):
                vector.wait_ge(SPl, G(t, f'sq{g}'))
                with nc.allow_low_precision(reason="sumsq partials in fp32; reduce math fp32"):
                    ev(f'red{g}', t, vector.tensor_reduce(
                        out=sumsq[:, g * 8:(g + 1) * 8],
                        in_=sq_sb[:, g * 512:(g + 1) * 512].rearrange("p (h d) -> p h d", d=64),
                        axis=AX.X, op=ALU.add))

            def e_varq(t):
                vector.wait_ge(SV, V(t, 'red1'))
                vector.wait_ge(SV, V(t, 'mean'))
                vsync(vector.tensor_tensor(out=m2[:, 0:16], in0=m_qk[t % 2][:, 0:16],
                                           in1=m_qk[t % 2][:, 0:16], op=ALU.mult))
                ev('varq', t, vector.scalar_tensor_tensor(out=var_qk[:, 0:16], in0=sumsq[:, 0:16],
                                                          scalar=1.0 / HD, in1=m2[:, 0:16],
                                                          op0=ALU.mult, op1=ALU.subtract))

            def e_vark(t):
                vector.wait_ge(SV, V(t, 'red3'))
                vsync(vector.tensor_tensor(out=m2[:, 16:32], in0=m_qk[t % 2][:, 16:32],
                                           in1=m_qk[t % 2][:, 16:32], op=ALU.mult))
                ev('vark', t, vector.scalar_tensor_tensor(out=var_qk[:, 16:32], in0=sumsq[:, 16:32],
                                                          scalar=1.0 / HD, in1=m2[:, 16:32],
                                                          op0=ALU.mult, op1=ALU.subtract))

            def e_qcen(t):
                vector.wait_ge(SS, S(t, 'evac1'))
                vector.wait_ge(SV, V(t, 'mean'))
                vector.wait_ge(SS, S(-1, 'sign'))
                ev('qcen', t, vector.tensor_tensor(
                    out=qkc[:, 0:1024].rearrange("p (h d) -> p h d", d=64),
                    in0=qk_sb[t % 2][:, 0:1024].rearrange("p (h d) -> p h d", d=64),
                    in1=m_qk[t % 2][:, 0:16, None].broadcast_to([128, 16, 64]),
                    op=ALU.subtract))

            def e_qmul(t):
                vector.wait_ge(SS, S(t, 'rsq'))
                vector.wait_ge(SV, V(t, 'qcen'))
                if t >= 1:
                    vector.wait_ge(ST, T(t - 1, 'qnTq'))
                ev('qmul', t, vector.tensor_tensor(
                    out=qkn[:, 0:1024].rearrange("p (h d) -> p h d", d=64),
                    in0=qkc[:, 0:1024].rearrange("p (h d) -> p h d", d=64),
                    in1=rstdqk[t % 2][:, 0:16, None].broadcast_to([128, 16, 64]),
                    op=ALU.mult))

            def e_kmul(t):
                vector.wait_ge(SS, S(t, 'rsk'))
                vector.wait_ge(SPl, G(t, 'kcen'))
                if t >= 1:
                    vector.wait_ge(ST, T(t - 1, 'qnTk'))
                ev('kmul', t, vector.tensor_tensor(
                    out=qkn[:, 1024:2048].rearrange("p (h d) -> p h d", d=64),
                    in0=qkc[:, 1024:2048].rearrange("p (h d) -> p h d", d=64),
                    in1=rstdqk[t % 2][:, 16:32, None].broadcast_to([128, 16, 64]),
                    op=ALU.mult))

            def e_qTe(t):
                vector.wait_ge(ST, T(t, 'qnTq'))
                vector.wait_ge(SP, 1)
                vsync(vector.tensor_scalar(
                    out=qTz[0:64, :].rearrange("p (a b) -> p a b", b=256)[:, :, 0:128],
                    in0=P_B[0:64, 0:1024].rearrange("p (a b) -> p a b", b=128),
                    scalar1=gb4[0:64, 0:1], scalar2=gb4[0:64, 1:2], op0=ALU.mult, op1=ALU.add))
                ev('qTe', t, vector.tensor_scalar(
                    out=qTz[64:128, :].rearrange("p (a b) -> p a b", b=256)[:, :, 128:256],
                    in0=P_B[64:128, 0:1024].rearrange("p (a b) -> p a b", b=128),
                    scalar1=gb4[64:128, 0:1], scalar2=gb4[64:128, 1:2], op0=ALU.mult, op1=ALU.add))

            def e_negM(t, h):
                if h == 0:
                    vector.wait_ge(ST, T(t, 'scores'))
                    if t >= 1:
                        vector.wait_ge(SS, S(t - 1, 'exp0'))
                else:
                    if t >= 1:
                        vector.wait_ge(SS, S(t - 1, 'exp1'))
                ev(f'negM{h}', t, vector.reduce_max(
                    out=negM[:, h:h + 1],
                    in_=P_C[:, h * 512:(h + 1) * 512], axis=AX.X, negate=True))

            def e_sum(t, h):
                vector.wait_ge(SS, S(t, f'exp{h}'))
                ev(f'sum{h}', t, vector.tensor_reduce(
                    out=S16[:, h * 8:(h + 1) * 8],
                    in_=E_sb[:, h * 512:(h + 1) * 512].rearrange("p (h d) -> p h d", d=64),
                    axis=AX.X, op=ALU.add))

            def e_rcp(t):
                vector.wait_ge(SV, V(t, 'sum1'))
                ev('rcp', t, vector.reciprocal(out=r16[:, :], in_=S16[:, :]))

            def e_normw(t):
                vector.wait_ge(SV, V(t, 'rcp'))
                ev('normw', t, vector.tensor_tensor(
                    out=E_sb[:, :].rearrange("p (h d) -> p h d", d=64),
                    in0=E_sb[:, :].rearrange("p (h d) -> p h d", d=64),
                    in1=r16[:, :, None].broadcast_to([128, 16, 64]), op=ALU.mult))

            def e_sb(t):
                vector.wait_ge(ST, T(t, 'proj'))
                if t >= 1:
                    vector.wait_ge(SPl, G(t - 1, 'gate2'))
                if t == 0:
                    vector.wait_ge(SC, 32)
                ev('sb', t, vector.scalar_tensor_tensor(out=s_bf[:, :], in0=P_C[:, :],
                                                        scalar=1.0, in1=bprojf[:, :],
                                                        op0=ALU.mult, op1=ALU.add))

            disp = {'qTzms': e_qTzms, 'PTzms': e_PTzms, 'bnst': e_bnst, 'aggr': e_aggr,
                    'xn': e_xn, 'mean': e_mean, 'varq': e_varq, 'vark': e_vark,
                    'qcen': e_qcen, 'qmul': e_qmul, 'kmul': e_kmul,
                    'qTe': e_qTe, 'rcp': e_rcp, 'normw': e_normw, 'sb': e_sb}
            for (t, name) in Vs:
                if name.startswith('wsum'):
                    e_wsum(t, int(name[4:]))
                elif name.startswith('red'):
                    e_red(t, int(name[3:]))
                elif name.startswith('negM'):
                    e_negM(t, int(name[4:]))
                elif name.startswith('sum'):
                    e_sum(t, int(name[3:]))
                else:
                    disp[name](t)

        @block.scalar
        def _(scalar):
            for c in range(4):
                scalar.dma_start(out=qkc[:, (c % 4) * 1024:((c % 4) + 1) * 1024],
                                 in_=wffnT_d[c * 128:(c + 1) * 128, :]).then_inc(SW3, 16)
            _spos = [0]

            def ev(name, t, ins):
                assert CS[(t, name)] == _spos[0] + 1, (name, t)
                _spos[0] += 1
                ins.then_inc(SS, 1)

            def e_sign(t):
                scalar.wait_ge(SW3, 64)
                for c in range(4):
                    scalar.activation(out=wfs[:, c * D:(c + 1) * D],
                                      in_=qkc[:, c * 1024:(c + 1) * 1024],
                                      func=AF.Sign).then_inc(SSi, 1)
                scalar.wait_ge(SSi, 4)
                for c in range(4, 8):
                    scalar.dma_start(out=qkc[:, (c % 4) * 1024:((c % 4) + 1) * 1024],
                                     in_=wffnT_d[c * 128:(c + 1) * 128, :]).then_inc(SW3, 16)
                scalar.wait_ge(SW3, 128)
                for c in range(4, 8):
                    ins = scalar.activation(out=wfs[:, c * D:(c + 1) * D],
                                            in_=qkc[:, (c % 4) * 1024:((c % 4) + 1) * 1024], func=AF.Sign)
                ev('sign', t, ins)

            def e_lnx(t):
                scalar.wait_ge(SV, V(t, 'aggr'))
                if t == 0:
                    scalar.wait_ge(SCc, 16)
                ev('lnx', t, scalar.activation(out=lnx[:, :], in_=mv[:, 1:2], func=AF.Ln, bias=cst[:, 0:1], scale=1.0))

            def e_rstdx(t):
                scalar.wait_ge(SS, S(t, 'lnx'))
                ev('rstdx', t, scalar.activation(out=rstdx[t % 2][:, :], in_=lnx[:, :], func=AF.Exp,
                                                 bias=cst[:, 1:2], scale=-0.5))

            def e_xnTe(t):
                scalar.wait_ge(ST, T(t, 'xnT'))
                if t >= 2:
                    scalar.wait_ge(ST, T(t - 2, 'qkv5'))
                ev('xnTe', t, scalar.copy(out=xnT[t % 2][:, :], in_=P_XT[:, :]))

            def e_evac(t, n):
                scalar.wait_ge(ST, T(t, f'qkv{n}'))
                if n == 0 and t >= 2:
                    scalar.wait_ge(SV, V(t - 2, 'kmul'))
                    scalar.wait_ge(SPl, G(t - 2, 'kcen'))
                if n == 4 and t >= 2:
                    scalar.wait_ge(ST, T(t - 2, 'av'))
                src = P_A[:, (n % 2) * 512:(n % 2 + 1) * 512]
                if n < 4:
                    ev(f'evac{n}', t, scalar.copy(out=qk_sb[t % 2][:, n * 512:(n + 1) * 512], in_=src))
                else:
                    ev(f'evac{n}', t, scalar.copy(out=v_sb[t % 2][:, (n - 4) * 512:(n - 3) * 512], in_=src))

            def e_lnq(t):
                scalar.wait_ge(SV, V(t, 'varq'))
                ev('lnq', t, scalar.activation(out=ln32[:, 0:16], in_=var_qk[:, 0:16],
                                               func=AF.Ln, bias=cst[:, 0:1], scale=1.0))

            def e_rsq(t):
                scalar.wait_ge(SS, S(t, 'lnq'))
                if t >= 2:
                    scalar.wait_ge(SV, V(t - 2, 'qmul'))
                ev('rsq', t, scalar.activation(out=rstdqk[t % 2][:, 0:16], in_=ln32[:, 0:16],
                                               func=AF.Exp, bias=cst[:, 1:2], scale=-0.5))

            def e_lnk(t):
                scalar.wait_ge(SV, V(t, 'vark'))
                ev('lnk', t, scalar.activation(out=ln32[:, 16:32], in_=var_qk[:, 16:32],
                                               func=AF.Ln, bias=cst[:, 0:1], scale=1.0))

            def e_rsk(t):
                scalar.wait_ge(SS, S(t, 'lnk'))
                if t >= 2:
                    scalar.wait_ge(SV, V(t - 2, 'kmul'))
                ev('rsk', t, scalar.activation(out=rstdqk[t % 2][:, 16:32], in_=ln32[:, 16:32],
                                               func=AF.Exp, bias=cst[:, 1:2], scale=-0.5))

            def e_PTe(t):
                scalar.wait_ge(ST, T(t, 'PT'))
                if t >= 1:
                    scalar.wait_ge(ST, T(t - 1, 'av'))
                scalar.copy(
                    out=PTz_sb[0:64, :].rearrange("p (h c) -> p h c", c=128)[:, :, 0:64],
                    in_=P_B[0:64, :].rearrange("p (h c) -> p h c", c=128)[:, :, 0:64])
                ev('PTe', t, scalar.copy(
                    out=PTz_sb[64:128, :].rearrange("p (h c) -> p h c", c=128)[:, :, 64:128],
                    in_=P_B[64:128, :].rearrange("p (h c) -> p h c", c=128)[:, :, 64:128]))

            def e_kTe(t):
                scalar.wait_ge(ST, T(t, 'qnTk'))
                if t == 0:
                    scalar.wait_ge(SP, 1)
                ev('kTe', t, scalar.activation(out=kT_sb[:, :], in_=P_B[:, 1024:2048], func=AF.Identity,
                                               bias=gb4[:, 3:4], scale=gb4[:, 2:3]))

            def e_exp(t, h):
                scalar.wait_ge(SV, V(t, f'negM{h}'))
                if h == 0 and t >= 1:
                    scalar.wait_ge(ST, T(t - 1, 'PT'))
                ev(f'exp{h}', t, scalar.activation(
                    out=E_sb[:, h * 512:(h + 1) * 512], in_=P_C[:, h * 512:(h + 1) * 512],
                    func=AF.Exp, bias=negM[:, h:h + 1], scale=1.0))

            def e_silu0(t):
                scalar.wait_ge(ST, T(t, 'ffn0'))
                if t >= 2:
                    scalar.wait_ge(SPl, G(t - 2, 'gate2'))
                ev('silu0', t, scalar.activation(out=g_sb[t % 2][:, 0:512], in_=P_ST[:, 0:512],
                                                 func=AF.Silu, bias=cst[:, 1:2]))

            def e_silu1(t):
                scalar.wait_ge(ST, T(t, 'ffn1'))
                ev('silu1', t, scalar.activation(out=g_sb[t % 2][:, 512:1024], in_=P_A[:, 512:1024],
                                                 func=AF.Silu, bias=cst[:, 1:2]))
                scalar.activation(out=dum[:, :], in_=cst[:, 0:1], func=AF.Ln)
                scalar.activation(out=dum[:, :], in_=cst[:, 0:1], func=AF.Exp)

            def e_attnTe(t):
                scalar.wait_ge(ST, T(t, 'av'))
                if t >= 1:
                    scalar.wait_ge(ST, T(t - 1, 'proj'))
                ev('attnTe', t, scalar.copy(out=at_sb[:, :], in_=P_C[:, 0:1024]))

            disp = {'sign': e_sign, 'lnx': e_lnx, 'rstdx': e_rstdx, 'xnTe': e_xnTe,
                    'kTe': e_kTe, 'lnq': e_lnq, 'rsq': e_rsq, 'lnk': e_lnk, 'rsk': e_rsk, 'PTe': e_PTe,
                    'attnTe': e_attnTe, 'silu0': e_silu0, 'silu1': e_silu1}
            for (t, name) in Ss:
                if name.startswith('evac'):
                    e_evac(t, int(name[4:]))
                elif name.startswith('exp'):
                    e_exp(t, int(name[3:]))
                else:
                    disp[name](t)

        @block.tensor
        def _(tensor):
            _tpos = [0]

            def ev(name, t, ins):
                assert CT[(t, name)] == _tpos[0] + 1, (name, t)
                _tpos[0] += 1
                ins.then_inc(ST, 1)

            def e_xnT(t):
                tensor.wait_ge(SV, V(t, 'xn'))
                if t == 0:
                    tensor.wait_ge(SCe, 16)
                if t >= 1:
                    tensor.wait_ge(SS, S(t - 1, 'xnTe'))
                for c in range(8):
                    ins = tensor.transpose(out=P_XT[:, c * 128:(c + 1) * 128],
                                           in_=xnbf[t % 2][:, c * 128:(c + 1) * 128], identity=eye[:, :])
                ev('xnT', t, ins)

            def e_qkv(t, n):
                tensor.wait_ge(SS, S(t, 'xnTe'))
                if n == 0:
                    if t >= 1:
                        tensor.wait_ge(SS, S(t - 1, 'evac4'))
                elif n == 1:
                    if t == 1:
                        tensor.wait_ge(SS, S(0, 'silu1'))
                    elif t >= 2:
                        tensor.wait_ge(SS, S(t - 1, 'evac5'))
                elif n <= 3:
                    tensor.wait_ge(SS, S(t, f'evac{n - 2}'))
                elif n == 4:
                    tensor.wait_ge(SS, S(t, 'evac2'))
                else:
                    if t == 0:
                        tensor.wait_ge(SS, S(0, 'evac3'))
                    else:
                        tensor.wait_ge(SS, S(t, 'silu1'))
                if t == 0:
                    tensor.wait_ge(SW1s[n], 16)
                dst = P_A[:, (n % 2) * 512:(n % 2 + 1) * 512]
                for k in range(8):
                    ins = tensor.matmul(out=dst, lhsT=xnT[t % 2][:, k * 128:(k + 1) * 128],
                                        rhs=wqkv[:, k, n * 512:(n + 1) * 512],
                                        start=(k == 0), stop=(k == 7))
                ev(f'qkv{n}', t, ins)

            def e_stats(t):
                if t == 0:
                    tensor.wait_ge(SV, V(-1, 'wsum7'))
                tensor.wait_ge(SS, S(t, 'xnTe'))
                if t >= 1:
                    tensor.wait_ge(SV, V(t - 1, 'mean'))
                    tensor.wait_ge(SS, S(t - 1, 'silu0'))
                for k in range(8):
                    ins = tensor.matmul(out=P_ST[:, 0:32], lhsT=xnT[t % 2][:, k * 128:(k + 1) * 128],
                                        rhs=wsum[:, k, :], start=(k == 0), stop=(k == 7))
                ev('stats', t, ins)

            def e_ffn(t, n):
                tensor.wait_ge(SS, S(-1, 'sign'))
                if n == 0:
                    tensor.wait_ge(SV, V(t, 'mean'))
                    dst = P_ST[:, 0:512]
                else:
                    if t == 0:
                        tensor.wait_ge(SS, S(0, 'evac5'))
                    else:
                        tensor.wait_ge(SS, S(t, 'evac3'))
                    dst = P_A[:, 512:1024]
                for k in range(8):
                    ins = tensor.matmul(out=dst,
                                        lhsT=xnT[t % 2][:, k * 128:(k + 1) * 128],
                                        rhs=wfs[:, k * D + n * 512:k * D + (n + 1) * 512],
                                        start=(k == 0), stop=(k == 7))
                ev(f'ffn{n}', t, ins)

            def e_qnTq(t):
                tensor.wait_ge(SV, V(t, 'qmul'))
                if t >= 1:
                    tensor.wait_ge(SS, S(t - 1, 'PTe'))
                for j in range(8):
                    ins = tensor.transpose(out=P_B[:, j * 128:(j + 1) * 128],
                                           in_=qkn[:, j * 128:(j + 1) * 128], identity=eye[:, :])
                ev('qnTq', t, ins)

            def e_qnTk(t):
                tensor.wait_ge(SV, V(t, 'kmul'))
                for j in range(8, 16):
                    ins = tensor.transpose(out=P_B[:, j * 128:(j + 1) * 128],
                                           in_=qkn[:, j * 128:(j + 1) * 128], identity=eye[:, :])
                ev('qnTk', t, ins)

            def e_scores(t):
                tensor.wait_ge(SV, V(t, 'qTe'))
                tensor.wait_ge(SS, S(t, 'kTe'))
                if t >= 1:
                    tensor.wait_ge(SV, V(t - 1, 'sb'))
                for h in range(NH):
                    fo = (h // 2) * 128
                    for w in range(2):
                        ins = tensor.matmul(
                            out=P_C[w * 64:(w + 1) * 64, h * 64:(h + 1) * 64],
                            lhsT=qTz[:, h * 128 + w * 64:h * 128 + (w + 1) * 64],
                            rhs=kT_sb[:, fo + w * 64:fo + (w + 1) * 64],
                            start=True, stop=True, tile_position=(0, w * 64))
                ev('scores', t, ins)

            def e_PT(t):
                tensor.wait_ge(SV, V(t, 'normw'))
                tensor.wait_ge(SV, V(t, 'qTe'))
                tensor.wait_ge(SS, S(t, 'kTe'))
                for h in range(NH):
                    for w in range(2):
                        ins = tensor.transpose(out=P_B[w * 64:(w + 1) * 64, h * 128 + w * 64:h * 128 + (w + 1) * 64],
                                               in_=E_sb[w * 64:(w + 1) * 64, h * 64:(h + 1) * 64],
                                               identity=eye[w * 64:w * 64 + 64, w * 64:w * 64 + 64])
                ev('PT', t, ins)

            def e_av(t):
                tensor.wait_ge(SS, S(t, 'PTe'))
                tensor.wait_ge(SS, S(t, 'evac5'))
                tensor.wait_ge(SS, S(t, 'exp1'))
                for h in range(NH):
                    ins = tensor.matmul(
                        out=P_C[(h % 2) * 64:(h % 2) * 64 + 64, (h // 2) * 128:(h // 2 + 1) * 128],
                        lhsT=v_sb[t % 2][:, h * 64:(h + 1) * 64],
                        rhs=PTz_sb[:, h * 128:(h + 1) * 128],
                        start=True, stop=True, tile_position=(0, (h % 2) * 64))
                ev('av', t, ins)

            def e_proj(t):
                tensor.wait_ge(SS, S(t, 'attnTe'))
                tensor.wait_ge(SS, S(t, 'exp1'))
                if t == 0:
                    tensor.wait_ge(SW4, 16)
                for b in range(2):
                    for k in range(8):
                        ins = tensor.matmul(out=P_C[:, b * 512:(b + 1) * 512],
                                            lhsT=at_sb[:, k * 128:(k + 1) * 128],
                                            rhs=wproj[:, k, b * 512:(b + 1) * 512],
                                            start=(k == 0), stop=(k == 7))
                ev('proj', t, ins)

            disp = {'xnT': e_xnT, 'stats': e_stats,
                    'qnTq': e_qnTq, 'qnTk': e_qnTk, 'scores': e_scores, 'PT': e_PT, 'av': e_av, 'proj': e_proj}
            for (t, name) in Ts:
                if name.startswith('qkv'):
                    e_qkv(t, int(name[3:]))
                elif name.startswith('ffn'):
                    e_ffn(t, int(name[3:]))
                else:
                    disp[name](t)
    return nc


def make_consts(gamma, beta):
    g2 = np.tile(gamma, 2)
    b2 = np.tile(beta, 2)
    return np.ascontiguousarray(np.stack([g2, b2, g2, b2], axis=1).astype(np.float32))


_NC_CACHE = None


def kernel(L0_attn_x, L0_attn_W_qkv, L0_attn_ln_gamma, L0_attn_ln_beta,
           L0_attn_W_proj, L0_attn_b_proj, L0_ffn_weight):
    global _NC_CACHE
    x = np.ascontiguousarray(np.asarray(L0_attn_x, dtype=np.float32)).reshape(16384, D)
    wqkv = np.ascontiguousarray(np.asarray(L0_attn_W_qkv, dtype=np.float32))
    wproj = np.ascontiguousarray(np.asarray(L0_attn_W_proj, dtype=np.float32))
    wffnT = np.ascontiguousarray(np.asarray(L0_ffn_weight, dtype=np.float32).T)
    gamma = np.asarray(L0_attn_ln_gamma, dtype=np.float32)
    beta = np.asarray(L0_attn_ln_beta, dtype=np.float32)
    bproj = np.ascontiguousarray(
        np.broadcast_to(np.asarray(L0_attn_b_proj, dtype=np.float32).reshape(1, D), (128, D)))
    gb4 = make_consts(gamma, beta)
    eye = np.eye(128, dtype=ml_dtypes.bfloat16)
    cst_np = np.zeros((128, 2), dtype=np.float32)
    cst_np[:, 0] = EPS

    if _NC_CACHE is None:
        _NC_CACHE = build_nc()
    nc = _NC_CACHE
    in_maps = []
    for c in range(NCORES):
        in_maps.append({
            "x": x[c * TOK:(c + 1) * TOK],
            "wqkv": wqkv, "wproj": wproj, "wffnT": wffnT,
            "bproj": bproj, "eye": eye, "cst": cst_np, "gb4": gb4,
        })
    res = run_bass_kernel_spmd(nc, in_maps, core_ids=list(range(NCORES)))
    out = np.concatenate([res.results[c]["y"] for c in range(NCORES)], axis=0)
    return out.reshape(4, 4096, D)


# version 53
# speedup vs baseline: 1.0252x; 1.0252x over previous
import numpy as np
import ml_dtypes
from contextlib import ExitStack

import concourse.bass as bass
import concourse.mybir as mybir
from concourse.bass_utils import run_bass_kernel_spmd

F32 = mybir.dt.float32
BF16 = mybir.dt.bfloat16
AF = mybir.ActivationFunctionType
ALU = mybir.AluOpType
AX = mybir.AxisListType

NCORES = 8
TOK = 2048
D = 1024
NT = TOK // 128
NH = 16
HD = 64
SCALE = 8.0 / HD
EPS = 1e-5


def schedule(nt):
    Vs = [(0, 'bnst'), (0, 'aggr'), (0, 'xn'), (-1, 'qTzms'), (-1, 'PTzms')]
    Vs += [(-1, f'wsum{c}') for c in range(8)]
    Ss = [(0, 'lnx'), (0, 'rstdx')]
    Ts = []
    Gs = []
    for i in range(nt):
        if i == 0:
            Vs += [(1, 'bnst'), (1, 'aggr'), (1, 'xn'), (0, 'red0'), (0, 'red1'),
                   (0, 'red2'), (0, 'red3'), (0, 'mean'), (0, 'varq'), (0, 'vark'),
                   (0, 'qcen'), (0, 'qmul'), (0, 'kmul'), (0, 'qTe')]
        else:
            Vs += [(i - 1, 'negM0'), (i - 1, 'negM1'), (i - 1, 'sum0'), (i - 1, 'sum1'),
                   (i - 1, 'rcp'), (i - 1, 'normw'),
                   (i, 'mean'), (i, 'red0'), (i, 'red1'), (i - 1, 'PTv'),
                   (i, 'varq'), (i, 'qcen'), (i, 'qmul')]
            if i + 1 < nt:
                Vs += [(i + 1, 'bnst'), (i + 1, 'aggr')]
            Vs += [(i, 'red2'), (i, 'red3'), (i, 'vark')]
            if i + 1 < nt:
                Vs += [(i + 1, 'xn')]
            Vs += [(i, 'kmul'), (i, 'qTe')]
            Vs += [(i - 1, 'sb')]
        if i == 0:
            Ss += [(0, 'xnTe'), (0, 'evac0'), (0, 'evac1'), (-1, 'sign'),
                   (1, 'lnx'), (1, 'rstdx'),
                   (0, 'evac2'), (0, 'evac3'),
                   (0, 'lnq'), (0, 'rsq'), (0, 'lnk'), (0, 'rsk'),
                   (0, 'evac4'), (0, 'evac5'), (0, 'silu0'), (0, 'silu1'), (0, 'kTe'),
                   (1, 'xnTe')]
        else:
            Ss += [(i - 1, 'exp0'), (i, 'evac0'), (i - 1, 'exp1'), (i, 'evac1'),
                   (i, 'evac2'), (i - 1, 'PTe'), (i, 'evac3'),
                   (i, 'lnq'), (i, 'rsq'), (i - 1, 'attnTe')]
            if i + 1 < nt:
                Ss += [(i + 1, 'lnx'), (i + 1, 'rstdx')]
            Ss += [(i, 'lnk'), (i, 'rsk'), (i, 'silu0'), (i, 'silu1'), (i, 'kTe')]
            if i + 1 < nt:
                Ss += [(i + 1, 'xnTe')]
            Ss += [(i, 'evac4'), (i, 'evac5')]
        if i == 0:
            Ts += [(0, 'xnT'), (0, 'qkv0'), (0, 'qkv1'), (0, 'qkv2'), (0, 'qkv3'),
                   (0, 'qkv4'), (0, 'qkv5'), (0, 'stats'), (0, 'ffn0'), (0, 'ffn1'),
                   (0, 'qnTq'), (0, 'qnTk'), (1, 'xnT'), (0, 'scores')]
        else:
            Ts += [(i, 'qkv0'), (i, 'stats'), (i, 'qkv1'), (i, 'qkv2'), (i, 'qkv3'),
                   (i - 1, 'PT'), (i, 'ffn0'), (i - 1, 'av'), (i, 'ffn1'),
                   (i, 'qnTq'), (i - 1, 'proj')]
            if i + 1 < nt:
                Ts += [(i + 1, 'xnT')]
            Ts += [(i, 'qnTk'), (i, 'qkv4'), (i, 'qkv5'), (i, 'scores')]
        if i == 0:
            Gs += [(0, 'sq0'), (0, 'sq1'), (0, 'sq2'), (0, 'sq3'), (0, 'kcen')]
        else:
            Gs += [(i, 'sq0'), (i, 'sq1'), (i, 'sq2'), (i, 'sq3'), (i, 'kcen'),
                   (i - 1, 'gate2'), (i - 1, 'gate')]
    t = nt - 1
    Vs += [(t, e) for e in ('negM0', 'negM1', 'sum0', 'sum1', 'rcp', 'normw', 'PTv', 'sb')]
    Ss += [(t, 'exp0'), (t, 'exp1'), (t, 'PTe'), (t, 'attnTe')]
    Ts += [(t, 'PT'), (t, 'av'), (t, 'proj')]
    Gs += [(t, 'gate2'), (t, 'gate')]
    return Vs, Ss, Ts, Gs


def build_nc(nt=NT):
    tok = nt * 128
    Vs, Ss, Ts, Gs = schedule(nt)
    CV = {ev: i + 1 for i, ev in enumerate(Vs)}
    CS = {ev: i + 1 for i, ev in enumerate(Ss)}
    CT = {ev: i + 1 for i, ev in enumerate(Ts)}
    CG = {ev: i + 1 for i, ev in enumerate(Gs)}

    def V(t, e):
        return CV[(t, e)]

    def S(t, e):
        return CS[(t, e)]

    def T(t, e):
        return CT[(t, e)]

    def G(t, e):
        return CG[(t, e)]

    nc = bass.Bass("TRN2", target_bir_lowering=False)
    x_d = nc.dram_tensor("x", [tok, D], F32, kind="ExternalInput").ap()
    wqkv_d = nc.dram_tensor("wqkv", [D, 3 * D], F32, kind="ExternalInput").ap()
    wproj_d = nc.dram_tensor("wproj", [D, D], F32, kind="ExternalInput").ap()
    wffnT_d = nc.dram_tensor("wffnT", [D, D], F32, kind="ExternalInput").ap()
    bproj_d = nc.dram_tensor("bproj", [128, D], F32, kind="ExternalInput").ap()
    eye_d = nc.dram_tensor("eye", [128, 128], BF16, kind="ExternalInput").ap()
    cst_d = nc.dram_tensor("cst", [128, 2], F32, kind="ExternalInput").ap()
    gb4_d = nc.dram_tensor("gb4", [128, 4], F32, kind="ExternalInput").ap()
    y_d = nc.dram_tensor("y", [tok, D], F32, kind="ExternalOutput").ap()

    with ExitStack() as st_:
        E = st_.enter_context
        wqkv = E(nc.sbuf_tensor("wqkv_sb", [128, 8, 3 * D], BF16))
        wsum = E(nc.sbuf_tensor("wsum_sb", [128, 8, 32], BF16))
        wproj = E(nc.sbuf_tensor("wproj_sb", [128, 8, D], BF16))
        wfs = E(nc.sbuf_tensor("wfs_sb", [128, 8 * D], BF16))
        bprojf = E(nc.sbuf_tensor("bprojf_sb", [128, D], F32))
        eye = E(nc.sbuf_tensor("eye_sb", [128, 128], BF16))
        cst = E(nc.sbuf_tensor("cst_sb", [128, 2], F32))
        gb4 = E(nc.sbuf_tensor("gb4_sb", [128, 4], F32))
        x_sb = [E(nc.sbuf_tensor(f"x_sb{i}", [128, D], F32)) for i in range(6)]
        out_sb = [E(nc.sbuf_tensor(f"out_sb{i}", [128, D], F32)) for i in range(2)]
        xnbf = [E(nc.sbuf_tensor(f"xnbf{i}", [128, D], BF16)) for i in range(2)]
        xnT = [E(nc.sbuf_tensor(f"xnT{i}", [128, D], BF16)) for i in range(2)]
        qk_sb = [E(nc.sbuf_tensor(f"qk_sb{i}", [128, 2048], F32)) for i in range(2)]
        v_sb = [E(nc.sbuf_tensor(f"v_sb{i}", [128, D], BF16)) for i in range(2)]
        g_sb = [E(nc.sbuf_tensor(f"g_sb{i}", [128, D], BF16)) for i in range(2)]
        sq_sb = E(nc.sbuf_tensor("sq_sb", [128, 2048], BF16))
        qkn = E(nc.sbuf_tensor("qkn", [128, 2048], BF16))
        qTz = E(nc.sbuf_tensor("qTz", [128, 2048], BF16))
        kT_sb = E(nc.sbuf_tensor("kT_sb", [128, 1024], BF16))
        E_sb = E(nc.sbuf_tensor("E_sb", [128, D], BF16))
        PTz_sb = E(nc.sbuf_tensor("PTz_sb", [128, 2048], BF16))
        at_sb = E(nc.sbuf_tensor("at_sb", [128, D], BF16))
        s_bf = E(nc.sbuf_tensor("s_bf", [128, D], BF16))
        sg_sb = E(nc.sbuf_tensor("sg_sb", [128, D], BF16))
        st6 = E(nc.sbuf_tensor("st6", [128, 12], F32))
        mv = E(nc.sbuf_tensor("mv", [128, 2], F32))
        lnx = E(nc.sbuf_tensor("lnx", [128, 1], F32))
        rstdx = [E(nc.sbuf_tensor(f"rstdx{i}", [128, 1], F32)) for i in range(2)]
        negmr = E(nc.sbuf_tensor("negmr", [128, 1], F32))
        sumsq = E(nc.sbuf_tensor("sumsq", [128, 32], F32))
        m_qk = [E(nc.sbuf_tensor(f"m_qk{i}", [128, 32], F32)) for i in range(2)]
        m2 = E(nc.sbuf_tensor("m2", [128, 32], F32))
        var_qk = E(nc.sbuf_tensor("var_qk", [128, 32], F32))
        ln32 = E(nc.sbuf_tensor("ln32", [128, 32], F32))
        rstdqk = [E(nc.sbuf_tensor(f"rstdqk{i}", [128, 32], F32)) for i in range(2)]
        negM = E(nc.sbuf_tensor("negM", [128, 2], F32))
        S16 = E(nc.sbuf_tensor("S16", [128, 16], F32))
        r16 = E(nc.sbuf_tensor("r16", [128, 16], F32))
        P_A = E(nc.psum_tensor("P_A", [128, 1024], F32))
        P_B = E(nc.psum_tensor("P_B", [128, 2048], BF16))
        P_C = E(nc.psum_tensor("P_C", [128, 1024], F32))
        P_ST = E(nc.psum_tensor("P_ST", [128, 512], F32))
        P_XT = E(nc.psum_tensor("P_XT", [128, 1024], BF16))
        qkc = E(nc.sbuf_tensor("qkc_sb", [128, 4096], F32))

        SV = E(nc.semaphore("SV"))
        SS = E(nc.semaphore("SS"))
        ST = E(nc.semaphore("ST"))
        SPl = E(nc.semaphore("SPl"))
        SP = E(nc.semaphore("SP"))
        SW1s = [E(nc.semaphore(f"SW1_{i}")) for i in range(6)]
        SW3 = E(nc.semaphore("SW3"))
        SW4 = E(nc.semaphore("SW4"))
        SXs = [E(nc.semaphore(f"SX{i}")) for i in range(6)]
        SYs = [E(nc.semaphore(f"SY{i}")) for i in range(2)]
        SC = E(nc.semaphore("SC"))
        SCc = E(nc.semaphore("SCc"))
        SCe = E(nc.semaphore("SCe"))
        SVi = E(nc.semaphore("SVi"))
        SSi = E(nc.semaphore("SSi"))
        block = E(nc.Block())

        @block.sync
        def _(sync):
            sync.dma_start(out=x_sb[0][:, :], in_=x_d[0:128, :]).then_inc(SXs[0], 16)
            sync.dma_start(out=cst[:, :], in_=cst_d).then_inc(SCc, 16)
            sync.dma_start(out=eye[:, :], in_=eye_d).then_inc(SCe, 16)
            sync.dma_start(out=gb4[:, :], in_=gb4_d).then_inc(SC, 16)
            sync.dma_start(out=bprojf[:, :], in_=bproj_d).then_inc(SC, 16)
            for u in range(1, min(6, nt)):
                sync.dma_start(out=x_sb[u % 6][:, :], in_=x_d[u * 128:(u + 1) * 128, :]).then_inc(SXs[u % 6], 16)
            for t in range(nt):
                sync.wait_ge(SPl, G(t, 'gate'))
                sync.wait_ge(SV, V(t, 'xn'))
                sync.dma_start(out=y_d[t * 128:(t + 1) * 128, :], in_=out_sb[t % 2][:, :]).then_inc(SYs[t % 2], 16)
                u = t + 6
                if u < nt:
                    sync.dma_start(out=x_sb[u % 6][:, :], in_=x_d[u * 128:(u + 1) * 128, :]).then_inc(SXs[u % 6], 16)
            sync.wait_ge(SYs[0], 16 * ((nt + 1) // 2))
            sync.wait_ge(SYs[1], 16 * (nt // 2))

        @block.gpsimd
        def _(gpsimd):
            wqkv_r = wqkv_d.rearrange("(c p) m -> p c m", p=128)
            for n in range(6):
                gpsimd.dma_start(out=wqkv[:, :, n * 512:(n + 1) * 512],
                                 in_=wqkv_r[:, :, n * 512:(n + 1) * 512]).then_inc(SW1s[n], 16)
            gpsimd.dma_start(out=wproj[:, :, :],
                             in_=wproj_d.rearrange("(c p) m -> p c m", p=128)).then_inc(SW4, 16)
            gpsimd.wait_ge(SC, 32)
            gpsimd.tensor_scalar(out=gb4[:, 0:2], in0=gb4[:, 0:2], scalar1=SCALE, scalar2=None, op0=ALU.mult).then_inc(SP, 1)

            _gpos = [0]

            def ev(name, t, ins):
                assert CG[(t, name)] == _gpos[0] + 1, (name, t)
                _gpos[0] += 1
                ins.then_inc(SPl, 1)

            def e_sq(t, g):
                gpsimd.wait_ge(SS, S(t, f'evac{g}'))
                if t >= 1:
                    gpsimd.wait_ge(SV, V(t - 1, f'red{g}'))
                src = qk_sb[t % 2][:, g * 512:(g + 1) * 512]
                ev(f'sq{g}', t, gpsimd.tensor_tensor(
                    out=sq_sb[:, g * 512:(g + 1) * 512], in0=src, in1=src, op=ALU.mult))

            def e_kcen(t):
                gpsimd.wait_ge(SV, V(t, 'mean'))
                gpsimd.wait_ge(SS, S(-1, 'sign'))
                ev('kcen', t, gpsimd.tensor_tensor(
                    out=qkc[:, 1024:2048].rearrange("p (h d) -> p h d", d=64),
                    in0=qk_sb[t % 2][:, 1024:2048].rearrange("p (h d) -> p h d", d=64),
                    in1=m_qk[t % 2][:, 16:32, None].broadcast_to([128, 16, 64]),
                    op=ALU.subtract))

            def e_gate2(t):
                gpsimd.wait_ge(SV, V(t, 'sb'))
                gpsimd.wait_ge(SS, S(t, 'silu1'))
                if t >= 1:
                    gpsimd.wait_ge(SPl, G(t - 1, 'gate'))
                ev('gate2', t, gpsimd.tensor_tensor(out=sg_sb[:, :], in0=s_bf[:, :],
                                                    in1=g_sb[t % 2][:, :], op=ALU.mult))

            def e_gate(t):
                xt = x_sb[t % 6]
                gpsimd.wait_ge(SPl, G(t, 'gate2'))
                if t >= 2:
                    gpsimd.wait_ge(SYs[t % 2], 16 * ((t - 2) // 2 + 1))
                ev('gate', t, gpsimd.tensor_tensor(out=out_sb[t % 2][:, :], in0=sg_sb[:, :],
                                                   in1=xt[:, :], op=ALU.add))

            gdisp = {'kcen': e_kcen, 'gate2': e_gate2, 'gate': e_gate}
            for (t, name) in Gs:
                if name.startswith('sq'):
                    e_sq(t, int(name[2:]))
                else:
                    gdisp[name](t)

        @block.vector
        def _(vector):
            vcnt = [0]

            def vsync(ins):
                ins.then_inc(SVi, 1)
                vcnt[0] += 1
                vector.wait_ge(SVi, vcnt[0])

            def ev(name, t, ins):
                assert CV[(t, name)] == _vpos[0] + 1, (name, t)
                _vpos[0] += 1
                ins.then_inc(SV, 1)
            _vpos = [0]

            def e_qTzms(t):
                ev('qTzms', t, vector.memset(qTz[:, :], 0.0))

            def e_PTzms(t):
                ev('PTzms', t, vector.memset(PTz_sb[:, :], 0.0))

            def e_wsum(t, c):
                if c == 0:
                    for n in range(4):
                        vector.wait_ge(SW1s[n], 16)
                with nc.allow_low_precision(reason="wsum rhs must be bf16 for matmul; reduce math is fp32"):
                    ev(f'wsum{c}', t, vector.tensor_reduce(
                        out=wsum[:, c, :],
                        in_=wqkv[:, c, 0:2048].rearrange("p (h d) -> p h d", d=64),
                        axis=AX.X, op=ALU.add))

            def e_bnst(t):
                xt = x_sb[t % 6]
                vector.wait_ge(SXs[t % 6], 16 * (t // 6 + 1))
                vector.bn_stats(out=st6[:, 0:6], in_=xt[:, 0:512])
                ev('bnst', t, vector.bn_stats(out=st6[:, 6:12], in_=xt[:, 512:1024]))

            def e_aggr(t):
                vector.wait_ge(SV, V(t, 'bnst'))
                if t >= 1:
                    vector.wait_ge(SS, S(t - 1, 'lnx'))
                ev('aggr', t, vector.bn_aggr(out=mv[:, :], in_=st6[:, :]))

            def e_xn(t):
                xt = x_sb[t % 6]
                vector.wait_ge(SS, S(t, 'rstdx'))
                if t >= 2:
                    vector.wait_ge(ST, T(t - 2, 'xnT'))
                vsync(vector.tensor_scalar(out=negmr[:, :], in0=mv[:, 0:1], scalar1=rstdx[t % 2][:, :],
                                           scalar2=-1.0, op0=ALU.mult, op1=ALU.mult))
                ev('xn', t, vector.tensor_scalar(out=xnbf[t % 2][:, :], in0=xt[:, :], scalar1=rstdx[t % 2][:, :],
                                                 scalar2=negmr[:, :], op0=ALU.mult, op1=ALU.add))

            def e_mean(t):
                vector.wait_ge(ST, T(t, 'stats'))
                if t >= 2:
                    vector.wait_ge(SPl, G(t - 2, 'kcen'))
                ev('mean', t, vector.tensor_scalar(out=m_qk[t % 2][:, :], in0=P_ST[:, 0:32],
                                                   scalar1=1.0 / HD, scalar2=None, op0=ALU.mult))

            def e_red(t, g):
                vector.wait_ge(SPl, G(t, f'sq{g}'))
                with nc.allow_low_precision(reason="sumsq partials in fp32; reduce math fp32"):
                    ev(f'red{g}', t, vector.tensor_reduce(
                        out=sumsq[:, g * 8:(g + 1) * 8],
                        in_=sq_sb[:, g * 512:(g + 1) * 512].rearrange("p (h d) -> p h d", d=64),
                        axis=AX.X, op=ALU.add))

            def e_varq(t):
                vector.wait_ge(SV, V(t, 'red1'))
                vector.wait_ge(SV, V(t, 'mean'))
                vsync(vector.tensor_tensor(out=m2[:, 0:16], in0=m_qk[t % 2][:, 0:16],
                                           in1=m_qk[t % 2][:, 0:16], op=ALU.mult))
                ev('varq', t, vector.scalar_tensor_tensor(out=var_qk[:, 0:16], in0=sumsq[:, 0:16],
                                                          scalar=1.0 / HD, in1=m2[:, 0:16],
                                                          op0=ALU.mult, op1=ALU.subtract))

            def e_vark(t):
                vector.wait_ge(SV, V(t, 'red3'))
                vsync(vector.tensor_tensor(out=m2[:, 16:32], in0=m_qk[t % 2][:, 16:32],
                                           in1=m_qk[t % 2][:, 16:32], op=ALU.mult))
                ev('vark', t, vector.scalar_tensor_tensor(out=var_qk[:, 16:32], in0=sumsq[:, 16:32],
                                                          scalar=1.0 / HD, in1=m2[:, 16:32],
                                                          op0=ALU.mult, op1=ALU.subtract))

            def e_qcen(t):
                vector.wait_ge(SS, S(t, 'evac1'))
                vector.wait_ge(SV, V(t, 'mean'))
                vector.wait_ge(SS, S(-1, 'sign'))
                ev('qcen', t, vector.tensor_tensor(
                    out=qkc[:, 0:1024].rearrange("p (h d) -> p h d", d=64),
                    in0=qk_sb[t % 2][:, 0:1024].rearrange("p (h d) -> p h d", d=64),
                    in1=m_qk[t % 2][:, 0:16, None].broadcast_to([128, 16, 64]),
                    op=ALU.subtract))

            def e_qmul(t):
                vector.wait_ge(SS, S(t, 'rsq'))
                vector.wait_ge(SV, V(t, 'qcen'))
                if t >= 1:
                    vector.wait_ge(ST, T(t - 1, 'qnTq'))
                ev('qmul', t, vector.tensor_tensor(
                    out=qkn[:, 0:1024].rearrange("p (h d) -> p h d", d=64),
                    in0=qkc[:, 0:1024].rearrange("p (h d) -> p h d", d=64),
                    in1=rstdqk[t % 2][:, 0:16, None].broadcast_to([128, 16, 64]),
                    op=ALU.mult))

            def e_kmul(t):
                vector.wait_ge(SS, S(t, 'rsk'))
                vector.wait_ge(SPl, G(t, 'kcen'))
                if t >= 1:
                    vector.wait_ge(ST, T(t - 1, 'qnTk'))
                ev('kmul', t, vector.tensor_tensor(
                    out=qkn[:, 1024:2048].rearrange("p (h d) -> p h d", d=64),
                    in0=qkc[:, 1024:2048].rearrange("p (h d) -> p h d", d=64),
                    in1=rstdqk[t % 2][:, 16:32, None].broadcast_to([128, 16, 64]),
                    op=ALU.mult))

            def e_qTe(t):
                vector.wait_ge(ST, T(t, 'qnTq'))
                vector.wait_ge(SP, 1)
                vsync(vector.tensor_scalar(
                    out=qTz[0:64, :].rearrange("p (a b) -> p a b", b=256)[:, :, 0:128],
                    in0=P_B[0:64, 0:1024].rearrange("p (a b) -> p a b", b=128),
                    scalar1=gb4[0:64, 0:1], scalar2=gb4[0:64, 1:2], op0=ALU.mult, op1=ALU.add))
                ev('qTe', t, vector.tensor_scalar(
                    out=qTz[64:128, :].rearrange("p (a b) -> p a b", b=256)[:, :, 128:256],
                    in0=P_B[64:128, 0:1024].rearrange("p (a b) -> p a b", b=128),
                    scalar1=gb4[64:128, 0:1], scalar2=gb4[64:128, 1:2], op0=ALU.mult, op1=ALU.add))

            def e_negM(t, h):
                if h == 0:
                    vector.wait_ge(ST, T(t, 'scores'))
                    if t >= 1:
                        vector.wait_ge(SS, S(t - 1, 'exp0'))
                else:
                    if t >= 1:
                        vector.wait_ge(SS, S(t - 1, 'exp1'))
                ev(f'negM{h}', t, vector.reduce_max(
                    out=negM[:, h:h + 1],
                    in_=P_C[:, h * 512:(h + 1) * 512], axis=AX.X, negate=True))

            def e_sum(t, h):
                vector.wait_ge(SS, S(t, f'exp{h}'))
                ev(f'sum{h}', t, vector.tensor_reduce(
                    out=S16[:, h * 8:(h + 1) * 8],
                    in_=E_sb[:, h * 512:(h + 1) * 512].rearrange("p (h d) -> p h d", d=64),
                    axis=AX.X, op=ALU.add))

            def e_rcp(t):
                vector.wait_ge(SV, V(t, 'sum1'))
                ev('rcp', t, vector.reciprocal(out=r16[:, :], in_=S16[:, :]))

            def e_normw(t):
                vector.wait_ge(SV, V(t, 'rcp'))
                ev('normw', t, vector.tensor_tensor(
                    out=E_sb[:, :].rearrange("p (h d) -> p h d", d=64),
                    in0=E_sb[:, :].rearrange("p (h d) -> p h d", d=64),
                    in1=r16[:, :, None].broadcast_to([128, 16, 64]), op=ALU.mult))

            def e_PTv(t):
                vector.wait_ge(ST, T(t, 'PT'))
                if t >= 1:
                    vector.wait_ge(ST, T(t - 1, 'av'))
                ev('PTv', t, vector.tensor_copy(
                    out=PTz_sb[64:128, :].rearrange("p (h c) -> p h c", c=128)[:, :, 64:128],
                    in_=P_B[64:128, :].rearrange("p (h c) -> p h c", c=128)[:, :, 64:128]))

            def e_sb(t):
                vector.wait_ge(ST, T(t, 'proj'))
                if t >= 1:
                    vector.wait_ge(SPl, G(t - 1, 'gate2'))
                if t == 0:
                    vector.wait_ge(SC, 32)
                ev('sb', t, vector.scalar_tensor_tensor(out=s_bf[:, :], in0=P_C[:, :],
                                                        scalar=1.0, in1=bprojf[:, :],
                                                        op0=ALU.mult, op1=ALU.add))

            disp = {'qTzms': e_qTzms, 'PTzms': e_PTzms, 'bnst': e_bnst, 'aggr': e_aggr,
                    'xn': e_xn, 'mean': e_mean, 'varq': e_varq, 'vark': e_vark,
                    'qcen': e_qcen, 'qmul': e_qmul, 'kmul': e_kmul, 'PTv': e_PTv,
                    'qTe': e_qTe, 'rcp': e_rcp, 'normw': e_normw, 'sb': e_sb}
            for (t, name) in Vs:
                if name.startswith('wsum'):
                    e_wsum(t, int(name[4:]))
                elif name.startswith('red'):
                    e_red(t, int(name[3:]))
                elif name.startswith('negM'):
                    e_negM(t, int(name[4:]))
                elif name.startswith('sum'):
                    e_sum(t, int(name[3:]))
                else:
                    disp[name](t)

        @block.scalar
        def _(scalar):
            for c in range(4):
                scalar.dma_start(out=qkc[:, (c % 4) * 1024:((c % 4) + 1) * 1024],
                                 in_=wffnT_d[c * 128:(c + 1) * 128, :]).then_inc(SW3, 16)
            _spos = [0]

            def ev(name, t, ins):
                assert CS[(t, name)] == _spos[0] + 1, (name, t)
                _spos[0] += 1
                ins.then_inc(SS, 1)

            def e_sign(t):
                scalar.wait_ge(SW3, 64)
                for c in range(4):
                    scalar.activation(out=wfs[:, c * D:(c + 1) * D],
                                      in_=qkc[:, c * 1024:(c + 1) * 1024],
                                      func=AF.Sign).then_inc(SSi, 1)
                scalar.wait_ge(SSi, 4)
                for c in range(4, 8):
                    scalar.dma_start(out=qkc[:, (c % 4) * 1024:((c % 4) + 1) * 1024],
                                     in_=wffnT_d[c * 128:(c + 1) * 128, :]).then_inc(SW3, 16)
                scalar.wait_ge(SW3, 128)
                for c in range(4, 8):
                    ins = scalar.activation(out=wfs[:, c * D:(c + 1) * D],
                                            in_=qkc[:, (c % 4) * 1024:((c % 4) + 1) * 1024], func=AF.Sign)
                ev('sign', t, ins)

            def e_lnx(t):
                scalar.wait_ge(SV, V(t, 'aggr'))
                if t == 0:
                    scalar.wait_ge(SCc, 16)
                ev('lnx', t, scalar.activation(out=lnx[:, :], in_=mv[:, 1:2], func=AF.Ln, bias=cst[:, 0:1], scale=1.0))

            def e_rstdx(t):
                scalar.wait_ge(SS, S(t, 'lnx'))
                ev('rstdx', t, scalar.activation(out=rstdx[t % 2][:, :], in_=lnx[:, :], func=AF.Exp,
                                                 bias=cst[:, 1:2], scale=-0.5))

            def e_xnTe(t):
                scalar.wait_ge(ST, T(t, 'xnT'))
                if t >= 2:
                    scalar.wait_ge(ST, T(t - 2, 'qkv5'))
                ev('xnTe', t, scalar.copy(out=xnT[t % 2][:, :], in_=P_XT[:, :]))

            def e_evac(t, n):
                scalar.wait_ge(ST, T(t, f'qkv{n}'))
                if n == 0 and t >= 2:
                    scalar.wait_ge(SV, V(t - 2, 'kmul'))
                    scalar.wait_ge(SPl, G(t - 2, 'kcen'))
                if n == 4 and t >= 2:
                    scalar.wait_ge(ST, T(t - 2, 'av'))
                src = P_A[:, (n % 2) * 512:(n % 2 + 1) * 512]
                if n < 4:
                    ev(f'evac{n}', t, scalar.copy(out=qk_sb[t % 2][:, n * 512:(n + 1) * 512], in_=src))
                else:
                    ev(f'evac{n}', t, scalar.copy(out=v_sb[t % 2][:, (n - 4) * 512:(n - 3) * 512], in_=src))

            def e_lnq(t):
                scalar.wait_ge(SV, V(t, 'varq'))
                ev('lnq', t, scalar.activation(out=ln32[:, 0:16], in_=var_qk[:, 0:16],
                                               func=AF.Ln, bias=cst[:, 0:1], scale=1.0))

            def e_rsq(t):
                scalar.wait_ge(SS, S(t, 'lnq'))
                if t >= 2:
                    scalar.wait_ge(SV, V(t - 2, 'qmul'))
                ev('rsq', t, scalar.activation(out=rstdqk[t % 2][:, 0:16], in_=ln32[:, 0:16],
                                               func=AF.Exp, bias=cst[:, 1:2], scale=-0.5))

            def e_lnk(t):
                scalar.wait_ge(SV, V(t, 'vark'))
                ev('lnk', t, scalar.activation(out=ln32[:, 16:32], in_=var_qk[:, 16:32],
                                               func=AF.Ln, bias=cst[:, 0:1], scale=1.0))

            def e_rsk(t):
                scalar.wait_ge(SS, S(t, 'lnk'))
                if t >= 2:
                    scalar.wait_ge(SV, V(t - 2, 'kmul'))
                ev('rsk', t, scalar.activation(out=rstdqk[t % 2][:, 16:32], in_=ln32[:, 16:32],
                                               func=AF.Exp, bias=cst[:, 1:2], scale=-0.5))

            def e_PTe(t):
                scalar.wait_ge(ST, T(t, 'PT'))
                if t >= 1:
                    scalar.wait_ge(ST, T(t - 1, 'av'))
                ev('PTe', t, scalar.copy(
                    out=PTz_sb[0:64, :].rearrange("p (h c) -> p h c", c=128)[:, :, 0:64],
                    in_=P_B[0:64, :].rearrange("p (h c) -> p h c", c=128)[:, :, 0:64]))

            def e_kTe(t):
                scalar.wait_ge(ST, T(t, 'qnTk'))
                if t == 0:
                    scalar.wait_ge(SP, 1)
                ev('kTe', t, scalar.activation(out=kT_sb[:, :], in_=P_B[:, 1024:2048], func=AF.Identity,
                                               bias=gb4[:, 3:4], scale=gb4[:, 2:3]))

            def e_exp(t, h):
                scalar.wait_ge(SV, V(t, f'negM{h}'))
                if h == 0 and t >= 1:
                    scalar.wait_ge(ST, T(t - 1, 'PT'))
                ev(f'exp{h}', t, scalar.activation(
                    out=E_sb[:, h * 512:(h + 1) * 512], in_=P_C[:, h * 512:(h + 1) * 512],
                    func=AF.Exp, bias=negM[:, h:h + 1], scale=1.0))

            def e_silu0(t):
                scalar.wait_ge(ST, T(t, 'ffn0'))
                if t >= 2:
                    scalar.wait_ge(SPl, G(t - 2, 'gate2'))
                ev('silu0', t, scalar.activation(out=g_sb[t % 2][:, 0:512], in_=P_ST[:, 0:512],
                                                 func=AF.Silu, bias=cst[:, 1:2]))

            def e_silu1(t):
                scalar.wait_ge(ST, T(t, 'ffn1'))
                ev('silu1', t, scalar.activation(out=g_sb[t % 2][:, 512:1024], in_=P_A[:, 512:1024],
                                                 func=AF.Silu, bias=cst[:, 1:2]))

            def e_attnTe(t):
                scalar.wait_ge(ST, T(t, 'av'))
                if t >= 1:
                    scalar.wait_ge(ST, T(t - 1, 'proj'))
                ev('attnTe', t, scalar.copy(out=at_sb[:, :], in_=P_C[:, 0:1024]))

            disp = {'sign': e_sign, 'lnx': e_lnx, 'rstdx': e_rstdx, 'xnTe': e_xnTe,
                    'kTe': e_kTe, 'lnq': e_lnq, 'rsq': e_rsq, 'lnk': e_lnk, 'rsk': e_rsk, 'PTe': e_PTe,
                    'attnTe': e_attnTe, 'silu0': e_silu0, 'silu1': e_silu1}
            for (t, name) in Ss:
                if name.startswith('evac'):
                    e_evac(t, int(name[4:]))
                elif name.startswith('exp'):
                    e_exp(t, int(name[3:]))
                else:
                    disp[name](t)

        @block.tensor
        def _(tensor):
            _tpos = [0]

            def ev(name, t, ins):
                assert CT[(t, name)] == _tpos[0] + 1, (name, t)
                _tpos[0] += 1
                ins.then_inc(ST, 1)

            def e_xnT(t):
                tensor.wait_ge(SV, V(t, 'xn'))
                if t == 0:
                    tensor.wait_ge(SCe, 16)
                if t >= 1:
                    tensor.wait_ge(SS, S(t - 1, 'xnTe'))
                for c in range(8):
                    ins = tensor.transpose(out=P_XT[:, c * 128:(c + 1) * 128],
                                           in_=xnbf[t % 2][:, c * 128:(c + 1) * 128], identity=eye[:, :])
                ev('xnT', t, ins)

            def e_qkv(t, n):
                tensor.wait_ge(SS, S(t, 'xnTe'))
                if n == 0:
                    if t >= 1:
                        tensor.wait_ge(SS, S(t - 1, 'evac4'))
                elif n == 1:
                    if t == 1:
                        tensor.wait_ge(SS, S(0, 'silu1'))
                    elif t >= 2:
                        tensor.wait_ge(SS, S(t - 1, 'evac5'))
                elif n <= 3:
                    tensor.wait_ge(SS, S(t, f'evac{n - 2}'))
                elif n == 4:
                    tensor.wait_ge(SS, S(t, 'evac2'))
                else:
                    if t == 0:
                        tensor.wait_ge(SS, S(0, 'evac3'))
                    else:
                        tensor.wait_ge(SS, S(t, 'silu1'))
                if t == 0:
                    tensor.wait_ge(SW1s[n], 16)
                dst = P_A[:, (n % 2) * 512:(n % 2 + 1) * 512]
                for k in range(8):
                    ins = tensor.matmul(out=dst, lhsT=xnT[t % 2][:, k * 128:(k + 1) * 128],
                                        rhs=wqkv[:, k, n * 512:(n + 1) * 512],
                                        start=(k == 0), stop=(k == 7))
                ev(f'qkv{n}', t, ins)

            def e_stats(t):
                if t == 0:
                    tensor.wait_ge(SV, V(-1, 'wsum7'))
                tensor.wait_ge(SS, S(t, 'xnTe'))
                if t >= 1:
                    tensor.wait_ge(SV, V(t - 1, 'mean'))
                    tensor.wait_ge(SS, S(t - 1, 'silu0'))
                for k in range(8):
                    ins = tensor.matmul(out=P_ST[:, 0:32], lhsT=xnT[t % 2][:, k * 128:(k + 1) * 128],
                                        rhs=wsum[:, k, :], start=(k == 0), stop=(k == 7))
                ev('stats', t, ins)

            def e_ffn(t, n):
                tensor.wait_ge(SS, S(-1, 'sign'))
                if n == 0:
                    tensor.wait_ge(SV, V(t, 'mean'))
                    dst = P_ST[:, 0:512]
                else:
                    if t == 0:
                        tensor.wait_ge(SS, S(0, 'evac5'))
                    else:
                        tensor.wait_ge(SS, S(t, 'evac3'))
                    dst = P_A[:, 512:1024]
                for k in range(8):
                    ins = tensor.matmul(out=dst,
                                        lhsT=xnT[t % 2][:, k * 128:(k + 1) * 128],
                                        rhs=wfs[:, k * D + n * 512:k * D + (n + 1) * 512],
                                        start=(k == 0), stop=(k == 7))
                ev(f'ffn{n}', t, ins)

            def e_qnTq(t):
                tensor.wait_ge(SV, V(t, 'qmul'))
                if t >= 1:
                    tensor.wait_ge(SS, S(t - 1, 'PTe'))
                    tensor.wait_ge(SV, V(t - 1, 'PTv'))
                for j in range(8):
                    ins = tensor.transpose(out=P_B[:, j * 128:(j + 1) * 128],
                                           in_=qkn[:, j * 128:(j + 1) * 128], identity=eye[:, :])
                ev('qnTq', t, ins)

            def e_qnTk(t):
                tensor.wait_ge(SV, V(t, 'kmul'))
                for j in range(8, 16):
                    ins = tensor.transpose(out=P_B[:, j * 128:(j + 1) * 128],
                                           in_=qkn[:, j * 128:(j + 1) * 128], identity=eye[:, :])
                ev('qnTk', t, ins)

            def e_scores(t):
                tensor.wait_ge(SV, V(t, 'qTe'))
                tensor.wait_ge(SS, S(t, 'kTe'))
                if t >= 1:
                    tensor.wait_ge(SV, V(t - 1, 'sb'))
                for h in range(NH):
                    fo = (h // 2) * 128
                    for w in range(2):
                        ins = tensor.matmul(
                            out=P_C[w * 64:(w + 1) * 64, h * 64:(h + 1) * 64],
                            lhsT=qTz[:, h * 128 + w * 64:h * 128 + (w + 1) * 64],
                            rhs=kT_sb[:, fo + w * 64:fo + (w + 1) * 64],
                            start=True, stop=True, tile_position=(0, w * 64))
                ev('scores', t, ins)

            def e_PT(t):
                tensor.wait_ge(SV, V(t, 'normw'))
                tensor.wait_ge(SV, V(t, 'qTe'))
                tensor.wait_ge(SS, S(t, 'kTe'))
                for h in range(NH):
                    for w in range(2):
                        ins = tensor.transpose(out=P_B[w * 64:(w + 1) * 64, h * 128 + w * 64:h * 128 + (w + 1) * 64],
                                               in_=E_sb[w * 64:(w + 1) * 64, h * 64:(h + 1) * 64],
                                               identity=eye[w * 64:w * 64 + 64, w * 64:w * 64 + 64])
                ev('PT', t, ins)

            def e_av(t):
                tensor.wait_ge(SS, S(t, 'PTe'))
                tensor.wait_ge(SV, V(t, 'PTv'))
                tensor.wait_ge(SS, S(t, 'evac5'))
                tensor.wait_ge(SS, S(t, 'exp1'))
                for h in range(NH):
                    ins = tensor.matmul(
                        out=P_C[(h % 2) * 64:(h % 2) * 64 + 64, (h // 2) * 128:(h // 2 + 1) * 128],
                        lhsT=v_sb[t % 2][:, h * 64:(h + 1) * 64],
                        rhs=PTz_sb[:, h * 128:(h + 1) * 128],
                        start=True, stop=True, tile_position=(0, (h % 2) * 64))
                ev('av', t, ins)

            def e_proj(t):
                tensor.wait_ge(SS, S(t, 'attnTe'))
                tensor.wait_ge(SS, S(t, 'exp1'))
                if t == 0:
                    tensor.wait_ge(SW4, 16)
                for b in range(2):
                    for k in range(8):
                        ins = tensor.matmul(out=P_C[:, b * 512:(b + 1) * 512],
                                            lhsT=at_sb[:, k * 128:(k + 1) * 128],
                                            rhs=wproj[:, k, b * 512:(b + 1) * 512],
                                            start=(k == 0), stop=(k == 7))
                ev('proj', t, ins)

            disp = {'xnT': e_xnT, 'stats': e_stats,
                    'qnTq': e_qnTq, 'qnTk': e_qnTk, 'scores': e_scores, 'PT': e_PT, 'av': e_av, 'proj': e_proj}
            for (t, name) in Ts:
                if name.startswith('qkv'):
                    e_qkv(t, int(name[3:]))
                elif name.startswith('ffn'):
                    e_ffn(t, int(name[3:]))
                else:
                    disp[name](t)
    return nc


def make_consts(gamma, beta):
    g2 = np.tile(gamma, 2)
    b2 = np.tile(beta, 2)
    return np.ascontiguousarray(np.stack([g2, b2, g2, b2], axis=1).astype(np.float32))


_NC_CACHE = None


def kernel(L0_attn_x, L0_attn_W_qkv, L0_attn_ln_gamma, L0_attn_ln_beta,
           L0_attn_W_proj, L0_attn_b_proj, L0_ffn_weight):
    global _NC_CACHE
    x = np.ascontiguousarray(np.asarray(L0_attn_x, dtype=np.float32)).reshape(16384, D)
    wqkv = np.ascontiguousarray(np.asarray(L0_attn_W_qkv, dtype=np.float32))
    wproj = np.ascontiguousarray(np.asarray(L0_attn_W_proj, dtype=np.float32))
    wffnT = np.ascontiguousarray(np.asarray(L0_ffn_weight, dtype=np.float32).T)
    gamma = np.asarray(L0_attn_ln_gamma, dtype=np.float32)
    beta = np.asarray(L0_attn_ln_beta, dtype=np.float32)
    bproj = np.ascontiguousarray(
        np.broadcast_to(np.asarray(L0_attn_b_proj, dtype=np.float32).reshape(1, D), (128, D)))
    gb4 = make_consts(gamma, beta)
    eye = np.eye(128, dtype=ml_dtypes.bfloat16)
    cst_np = np.zeros((128, 2), dtype=np.float32)
    cst_np[:, 0] = EPS

    if _NC_CACHE is None:
        _NC_CACHE = build_nc()
    nc = _NC_CACHE
    in_maps = []
    for c in range(NCORES):
        in_maps.append({
            "x": x[c * TOK:(c + 1) * TOK],
            "wqkv": wqkv, "wproj": wproj, "wffnT": wffnT,
            "bproj": bproj, "eye": eye, "cst": cst_np, "gb4": gb4,
        })
    res = run_bass_kernel_spmd(nc, in_maps, core_ids=list(range(NCORES)))
    out = np.concatenate([res.results[c]["y"] for c in range(NCORES)], axis=0)
    return out.reshape(4, 4096, D)
